# version 03 (final)
# baseline (speedup 1.0000x reference)
import numpy as np
import ml_dtypes
import concourse.bass as bass
import concourse.mybir as mybir
from concourse.bass_utils import run_bass_kernel_spmd

F32 = mybir.dt.float32
BF16 = mybir.dt.bfloat16
I8 = mybir.dt.int8
AF = mybir.ActivationFunctionType
ALU = mybir.AluOpType
AX = mybir.AxisListType
BF = ml_dtypes.bfloat16

P = 128
NTOK = 2048
NT = NTOK // P
D = 2048
DC = D // P
TSPLIT = 1024

N_SUBLAYERS = 6
ATTN_MAXPHASE = 99
ATTN_NHEADS = 16
ATTN_SCANPART = 99
ATTN_NBLK = 8
FFN_MAXPHASE = 99
PANEL_LIST = None
DBG_TAP = None
PANEL = 1024
RND = float(1.5 * 2.0 ** 23)
RS2K = float(1.0 / np.sqrt(np.float32(2048.0)))
RS4K = float(1.0 / np.sqrt(np.float32(4096.0)))

R_S0 = 0
R_S1 = 1024
R_MX = R_S0
R_MY = R_S1
R_SY = R_S0
R_SCTMP = R_S1
R_SCLROW = R_S0
RB_SC = 0
RB_BETA = 1024
RB_ALPHA = 2048
RB_A1SQ = 3072


class Em:
    def __init__(self, nc):
        self.nc = nc
        self.q = {k: [] for k in "tvsgy"}
        self._sems = {}
        self.cnt = {}

    def sem(self, name):
        if name not in self._sems:
            self._sems[name] = self.nc.ctx.enter_context(self.nc.semaphore(name))
            self.cnt[name] = 0
        return self._sems[name]

    def emit(self, qn, fn=None, waits=(), inc=None):
        waits = [(self.sem(n), v) for (n, v) in waits if v > 0]
        incs = None
        after = None
        if inc is not None:
            n, amt = inc
            s = self.sem(n)
            self.cnt[n] += amt
            after = self.cnt[n]
            incs = (s, amt)

        def thunk(eng, waits=waits, fn=fn, incs=incs):
            for (s, v) in waits:
                eng.wait_ge(s, v)
            if fn is not None:
                inst = fn(eng)
                if incs is not None:
                    assert inst is not None
                    inst.then_inc(*incs)
            else:
                assert incs is None

        self.q[qn].append(thunk)
        return after

    def val(self, name):
        return self.cnt.get(name, 0)

    def dma(self, qn, out, in_, waits=(), **kw):
        sems = {"g": "gd", "y": "yd"}
        return self.emit(qn, lambda e: e.dma_start(out, in_, **kw), waits,
                         inc=(sems[qn], 16))

    def replay(self, block):
        for attr, qn in [("tensor", "t"), ("vector", "v"), ("scalar", "s"),
                         ("gpsimd", "g"), ("sync", "y")]:
            thunks = self.q[qn]

            def body(e, thunks=thunks):
                for f in thunks:
                    f(e)

            getattr(block, attr)(body)


def tabT(sh):
    p = np.arange(P)[:, None] + 128 * sh
    j = np.arange(256)[None, :]
    d = (j - p).astype(np.float32)
    return np.where(d >= 0, d, np.float32(30000.0)).astype(BF)


def host_tables():
    h = np.array([[1.0]], dtype=np.float32)
    while h.shape[0] < P:
        h = np.block([[h, h], [h, -h]])
    dim = 128
    inv = 1.0 / 10000.0 ** (np.arange(0, dim, 2, dtype=np.float32) / dim)
    t = np.arange(NTOK, dtype=np.float32)
    fr = t[:, None] * inv[None, :]
    emb = np.concatenate([fr, fr], axis=-1)
    cos = np.ascontiguousarray(np.cos(emb).T.astype(np.float32))
    sin = np.ascontiguousarray(np.sin(emb).T.astype(np.float32))
    rsd = np.float32(1.0 / np.sqrt(np.float32(dim)))
    s2 = np.concatenate([-sin[0:64], sin[64:128]], axis=0)
    return {
        "h128": h.astype(BF),
        "ident": np.eye(P, dtype=np.float32).astype(BF),
        "onescol": np.ones((1, P), dtype=np.float32).astype(BF),
        "tri": np.where(np.arange(P)[None, :] > np.arange(P)[:, None],
                        np.float32(-10000.0), np.float32(0.0)).astype(BF),
        "arr1": np.broadcast_to(np.arange(1, 257, dtype=np.float32),
                                (P, 256)).astype(BF),
        "karrc": (256.0 - (np.arange(2)[None, :] * P + np.arange(P)[:, None] + 1)
                  ).astype(np.float32),
        "fconst": np.broadcast_to(np.array(
            [1e-5, 1e-6, 0.0, 256.0], dtype=np.float32), (P, 4)).copy(),
        "tabT0": tabT(0), "tabT1": tabT(1),
        "cosqb": (cos * rsd).astype(BF), "s2qb": (s2 * rsd).astype(BF),
        "coskb": cos.astype(BF), "s2kb": s2.astype(BF),
    }

CONST_SHAPES = {
    "h128": ([P, P], BF16), "ident": ([P, P], BF16), "onescol": ([1, P], BF16),
    "tri": ([P, P], BF16), "arr1": ([P, 256], BF16), "karrc": ([P, 2], F32),
    "fconst": ([P, 4], F32),
    "cosqb": ([P, NTOK], BF16), "s2qb": ([P, NTOK], BF16),
    "coskb": ([P, NTOK], BF16), "s2kb": ([P, NTOK], BF16),
    "tabT0": ([P, 256], BF16), "tabT1": ([P, 256], BF16),
}
LATER_CONSTS = ("cosqb", "s2qb", "coskb", "s2kb")


class B:
    pass


def alloc_all(nc, em):
    b = B()
    b.nc, b.em = nc, em
    b.io = {}
    b.sb = {}

    def din(name, shape, dtype):
        b.io[name] = nc.dram_tensor(name, list(shape), dtype, kind="ExternalInput").ap()

    def dout(name, shape, dtype):
        b.io[name] = nc.dram_tensor(name, list(shape), dtype, kind="ExternalOutput").ap()

    def dtmp(name, shape, dtype):
        b.io[name] = nc.dram_tensor(name, list(shape), dtype, kind="Internal").ap()

    def sb(name, shape, dtype):
        b.sb[name] = nc.ctx.enter_context(nc.sbuf_tensor(name, list(shape), dtype))[:]

    def ps(name, shape, dtype=F32):
        b.sb[name] = nc.ctx.enter_context(nc.psum_tensor(name, list(shape), dtype))[:]

    din("xin", [NTOK, D], F32)
    dout("xout", [NTOK, D], F32)
    dout("dbg", [P, 32 * 1024], F32)
    din("wupT", [D, 4096], F32)
    din("wdownT", [4096, D], F32)
    din("wqkvuT", [D, 4 * D], F32)
    din("woutT", [D, D], F32)
    din("slope", [1, 16], F32)
    din("l1wg", [D, 8192], I8)
    din("l1wu", [D, 8192], I8)
    din("l1wd", [8192, D], I8)
    din("l1sg", [P, 64], F32)
    din("l1su", [P, 64], F32)
    din("l1sd", [1, D], F32)
    din("l1wq", [D, D], F32)
    din("l1wdn", [D, 512], F32)
    din("l1wk", [512, D], F32)
    din("l1wv", [512, D], F32)
    din("l1wo", [D, D], F32)
    din("l1bq", [P, 16], F32)
    din("l1bo", [1, D], F32)
    dtmp("tupS", [D, 4096], BF16)
    dtmp("tdownS", [4096, D], BF16)
    dtmp("ptmp", [P, 16 * 2048], BF16)
    for nm, (shape, dt) in CONST_SHAPES.items():
        din(nm, shape, dt)

    sb("x", [P, NT * D], BF16)
    sb("big1", [P, 16 * 1024], BF16)
    sb("big2", [P, 32 * 1024], BF16)
    sb("junkb", [P, 1024], BF16)
    sb("spc", [P, 128], F32)
    sb("dscb", [P, 1024], BF16)
    sb("tmpf", [P, 1024], F32)
    sb("srep", [P, 1024], BF16)
    sb("lntm", [P, D], BF16)
    sb("rowsf", [1, 2 * 1024], F32)
    sb("rowsb", [1, 4 * 1024], BF16)
    sb("s1c", [P, 2], F32)
    sb("s2c", [P, 2], F32)
    sb("mvr", [P, 12], F32)
    sb("wacc", [P, 2], F32)
    sb("scl", [P, 4], F32)
    sb("lnx", [P, 4], F32)
    sb("slprow", [1, 16], F32)
    sb("slpn", [P, 16], F32)
    sb("bdec", [P, 16], F32)
    sb("ssq", [P, 48], F32)
    sb("ddt", [P, 512], BF16)
    sb("kdc", [P, 2], F32)
    sb("kvf", [P, P], F32)
    sb("wti", [P, 4096], BF16)
    sb("sdrep", [P, D], BF16)
    sb("l1s", [P, 128], F32)
    for nm, (shape, dt) in CONST_SHAPES.items():
        if nm in LATER_CONSTS:
            continue
        b.sb[nm] = nc.ctx.enter_context(
            nc.sbuf_tensor("c_" + nm, list(shape), dt))[:]

    b.sb["wfp"] = b.sb["big2"].bitcast(F32)[:, 0:4096]
    b.sb["sqf"] = b.sb["big2"].bitcast(F32)[:, 8192:10240]
    b.sb["outst"] = b.sb["big2"].bitcast(F32)[:, 4096:6144]

    ps("pA", [P, 512], F32)
    ps("pB", [P, 512], F32)
    ps("pC", [P, 1024], F32)
    ps("pD", [P, 1024], F32)
    ps("pT", [P, 512], BF16)
    return b


def vsp(b):
    b.em.emit("v", lambda e: e.tensor_copy(b.sb["spc"][:, 0:64],
                                           b.sb["spc"][:, 64:128]))


def register_consts(b):
    fc = b.sb["fconst"]
    for i, val in enumerate([1e-5, 1e-6, 0.0, 256.0]):
        b.nc.const_aps.register(F32, val, fc[:, i:i + 1])


def load_consts_and_x(b):
    em = b.em
    for nm in CONST_SHAPES:
        if nm in LATER_CONSTS:
            continue
        em.dma("y", b.sb[nm], b.io[nm])
    xin3 = b.io["xin"].rearrange("(j p) d -> p j d", p=P)
    x3 = b.sb["x"].rearrange("p (j d) -> p j d", d=D)
    em.dma("g", x3, xin3)
    return {"yd": em.val("yd"), "gd": em.val("gd")}


def l0ffn_weight_prep(b, ready):
    em = b.em
    wfp = b.sb["wfp"]
    wacc = b.sb["wacc"]
    tmpf = b.sb["tmpf"]
    scl = b.sb["scl"]
    pA = b.sb["pA"]
    rows = b.sb["rowsf"]

    em.emit("v", lambda e: e.memset(wacc[:, 0:2], 0.0),
            waits=[("yd", ready["yd"])], inc=("v", 1))

    def sum_pass(src, col, nchunks, width):
        for i in range(nchunks):
            ld = em.dma("g", wfp[:, :width], src[i * P:(i + 1) * P, :],
                        waits=[("v", em.val("v"))])
            em.emit("v", lambda e, width=width, i=i: e.reduce_sum(
                tmpf[:, i % 2:i % 2 + 1], wfp[:, :width], axis=AX.X,
                apply_absolute_value=True), waits=[("gd", ld)],
                inc=("v", 1))
            if i > 0:
                em.emit("v", lambda e, col=col, j=(i - 1) % 2: e.tensor_add(
                    wacc[:, col:col + 1], wacc[:, col:col + 1],
                    tmpf[:, j:j + 1]))
        vsp(b)
        em.emit("v", lambda e, col=col, i=nchunks - 1: e.tensor_add(
            wacc[:, col:col + 1], wacc[:, col:col + 1],
            tmpf[:, i % 2:i % 2 + 1]), inc=("v", 1))

    sum_pass(b.io["wupT"], 0, DC, 4096)
    sum_pass(b.io["wdownT"], 1, 32, 2048)

    em.emit("v", lambda e: e.memset(tmpf[:, 4:5], 1.0), inc=("v", 1))
    mm = em.emit("t", lambda e: e.matmul(pA[0:1, 0:2], tmpf[:, 4:5],
                                         wacc[:, 0:2], start=True, stop=True),
                 waits=[("v", em.val("v"))], inc=("t", 1))
    ev = em.emit("s", lambda e: e.copy(rows[0:1, R_SCLROW:R_SCLROW + 2],
                                       pA[0:1, 0:2]),
                 waits=[("t", mm)], inc=("s", 1))
    nelem = float(D * 4096)
    em.emit("v", lambda e: e.tensor_scalar(
        rows[0:1, R_SCLROW:R_SCLROW + 2], rows[0:1, R_SCLROW:R_SCLROW + 2],
        1.0 / nelem, 1e-6, ALU.mult, ALU.max), waits=[("s", ev)])
    vsp(b)
    em.emit("v", lambda e: e.reciprocal(
        rows[0:1, R_SCLROW + 2:R_SCLROW + 4], rows[0:1, R_SCLROW:R_SCLROW + 2]),
        inc=("v", 1))
    em.emit("v", lambda e: e.memset(rows[0:1, R_SCLROW + 8:R_SCLROW + 8 + P],
                                    1.0), inc=("v", 1))
    mm2 = em.emit("t", lambda e: e.matmul(
        pA[:, 8:12], rows[0:1, R_SCLROW + 8:R_SCLROW + 8 + P],
        rows[0:1, R_SCLROW:R_SCLROW + 4], start=True, stop=True),
        waits=[("v", em.val("v"))], inc=("t", 1))
    em.emit("s", lambda e: e.copy(scl[:, 0:4], pA[:, 8:12]),
            waits=[("t", mm2)], inc=("s", 1))
    s_scl = em.val("s")

    def quant_pass(src, dst, rcol, nchunks, width):
        hw = width // 2
        halves = [(0, hw), (hw, width)]
        for i in range(nchunks):
            ld = em.dma("g", wfp[:, :width], src[i * P:(i + 1) * P, :],
                        waits=[("v", em.val("v"))])
            for (a, z) in halves:
                em.emit("v", lambda e, a=a, z=z, rcol=rcol: e.tensor_scalar(
                    wfp[:, a:z], wfp[:, a:z], scl[:, rcol:rcol + 1], RND,
                    ALU.mult, ALU.add),
                    waits=[("gd", ld), ("s", s_scl)] if a == 0 else [])
            for (a, z) in halves:
                em.emit("v", lambda e, a=a, z=z: e.tensor_scalar(
                    wfp[:, a:z], wfp[:, a:z], -RND, None, ALU.add))
            for (a, z) in halves:
                em.emit("v", lambda e, a=a, z=z: e.tensor_scalar(
                    b.sb["big1"][:, a:z], wfp[:, a:z], -1.0, 1.0,
                    ALU.max, ALU.min),
                    waits=[("gd", em.val("gd"))] if a == 0 else [],
                    inc=("v", 1) if a != 0 else None)
            em.dma("g", dst[i * P:(i + 1) * P, :], b.sb["big1"][:, :width],
                   waits=[("v", em.val("v"))])

    quant_pass(b.io["wupT"], b.io["tupS"], 2, DC, 4096)
    quant_pass(b.io["wdownT"], b.io["tdownS"], 3, 32, 2048)
    return {"gd_spill": em.val("gd")}


def ln_tile(b, xt, col, wait_apply_s, attn=False):
    em = b.em
    s1 = b.sb["s1c"][:, col:col + 1]
    s2 = b.sb["s2c"][:, col:col + 1]
    mvr = b.sb["mvr"]
    c6 = col * 6
    mean = mvr[:, c6 + 0:c6 + 1]
    negm2 = mvr[:, c6 + 1:c6 + 2]
    var = mvr[:, c6 + 2:c6 + 3]
    sd = mvr[:, c6 + 3:c6 + 4]
    rr = mvr[:, c6 + 4:c6 + 5]
    bb = mvr[:, c6 + 5:c6 + 6]
    if attn:
        tmpf = b.sb["tmpf"]
        lnx = b.sb["lnx"]
        em.emit("v", lambda e: e.reduce_sum(s1, xt, axis=AX.X),
                waits=[("s", wait_apply_s)])
        sa = em.emit("s", lambda e: e.activation(
            tmpf, xt[:, 0:1024], AF.Square, bias=0.0, scale=1.0),
            waits=[("t", em.val("t")), ("v", em.val("v"))], inc=("s", 1))
        ra = em.emit("v", lambda e: e.reduce_sum(
            lnx[:, 0:1], tmpf, axis=AX.X), waits=[("s", sa)], inc=("v", 1))
        sbq = em.emit("s", lambda e: e.activation(
            tmpf, xt[:, 1024:2048], AF.Square, bias=0.0, scale=1.0),
            waits=[("v", ra)], inc=("s", 1))
        em.emit("v", lambda e: e.reduce_sum(
            lnx[:, 1:2], tmpf, axis=AX.X), waits=[("s", sbq)])
        vsp(b)
        em.emit("v", lambda e: e.tensor_add(s2, lnx[:, 0:1], lnx[:, 1:2]))
        vsp(b)
    else:
        sqf = b.sb["sqf"]
        ssq = em.emit("s", lambda e: e.activation(sqf, xt, AF.Square,
                                                  bias=0.0, scale=1.0),
                      waits=[("t", em.val("t")), ("v", em.val("v"))],
                      inc=("s", 1))
        em.emit("v", lambda e: e.reduce_sum(s1, xt, axis=AX.X),
                waits=[("s", wait_apply_s)])
        em.emit("v", lambda e: e.reduce_sum(s2, sqf, axis=AX.X),
                waits=[("s", ssq)])
    em.emit("v", lambda e: e.tensor_scalar(mean, s1, 1.0 / D, None, ALU.mult))
    vsp(b)
    em.emit("v", lambda e: e.scalar_tensor_tensor(
        negm2, mean, -1.0, mean, ALU.mult, ALU.mult))
    vsp(b)
    em.emit("v", lambda e: e.scalar_tensor_tensor(
        var, s2, 1.0 / D, negm2, ALU.mult, ALU.add), inc=("v", 1))
    vv = em.val("v")
    sq = em.emit("s", lambda e: e.activation(sd, var, AF.Sqrt, bias=1e-5,
                                             scale=1.0),
                 waits=[("v", vv)], inc=("s", 1))
    em.emit("v", lambda e: e.reciprocal(rr, sd), waits=[("s", sq)])
    vsp(b)
    em.emit("v", lambda e: e.tensor_scalar(bb, mean, rr, -1.0,
                                           ALU.mult, ALU.mult), inc=("v", 1))
    vv2 = em.val("v")
    ap = em.emit("s", lambda e: e.activation(b.sb["lntm"][:, :], xt,
                                             AF.Identity, bias=bb, scale=rr),
                 waits=[("v", vv2)], inc=("s", 1))
    return ap


def transpose_tile_to_fm(b, dst, dstcols, j, wait_s):
    em = b.em
    ident = b.sb["ident"]
    pT = b.sb["pT"]
    lntm = b.sb["lntm"]
    for g in range(4):
        def tr(e, g=g):
            inst = None
            for k in range(4):
                c = g * 4 + k
                inst = e.transpose(pT[:, k * P:(k + 1) * P],
                                   lntm[:, c * P:(c + 1) * P], ident)
            return inst
        w = [("s", wait_s if g == 0 else em.val("s"))]
        tt = em.emit("t", tr, waits=w, inc=("t", 1))

        def ev(e, g=g, j=j):
            inst = None
            for k in range(4):
                c = g * 4 + k
                inst = e.copy(dst[:, c * dstcols + j * P: c * dstcols + (j + 1) * P],
                              pT[:, k * P:(k + 1) * P])
            return inst
        em.emit("s", ev, waits=[("t", tt)], inc=("s", 1))
    return em.val("s")


def fwht_chunks(b, buf, nchunks, ntok):
    em = b.em
    tA1 = b.sb["junkb"][:, 0:ntok]
    tA2 = b.sb["dscb"][:, 0:ntok]
    tB1 = b.sb["lntm"][:, 0:ntok]
    tB2 = b.sb["lntm"][:, ntok:2 * ntok] if ntok <= 1024 else None
    h = 1
    while h < nchunks:
        pairs = [(g * 2 * h + i, g * 2 * h + h + i)
                 for g in range(nchunks // (2 * h)) for i in range(h)]
        for p in range(0, len(pairs), 2):
            (a1, c1), (a2, c2) = pairs[p], pairs[p + 1]
            a1 *= ntok; c1 *= ntok; a2 *= ntok; c2 *= ntok
            em.emit("v", lambda e, a=a1: e.tensor_copy(tA1, buf[:, a:a + ntok]))
            em.emit("v", lambda e, c=c1: e.tensor_copy(tA2, buf[:, c:c + ntok]))
            em.emit("v", lambda e, a=a2: e.tensor_copy(tB1, buf[:, a:a + ntok]))
            em.emit("v", lambda e, c=c2: e.tensor_copy(tB2, buf[:, c:c + ntok]))
            em.emit("v", lambda e, a=a1: e.tensor_add(buf[:, a:a + ntok], tA1, tA2))
            em.emit("v", lambda e, c=c1: e.tensor_sub(buf[:, c:c + ntok], tA1, tA2))
            em.emit("v", lambda e, a=a2: e.tensor_add(buf[:, a:a + ntok], tB1, tB2))
            last = (h * 2 >= nchunks and p + 2 >= len(pairs))
            em.emit("v", lambda e, c=c2: e.tensor_sub(buf[:, c:c + ntok], tB1, tB2),
                    inc=("v", 1) if last else None)
        h *= 2
    return em.val("v")


def quant_fm(b, buf, nchunks, ntok, alpha_off, rsn, wscale_ap, prerow):
    em = b.em
    rows = b.sb["rowsf"]
    mxA = b.sb["dscb"][:, 0:ntok]
    mxB = b.sb["junkb"][:, 0:ntok]
    mnA = b.sb["srep"][:, 0:ntok]
    mnB = b.sb["tmpf"][:, 0:ntok]
    ident = b.sb["ident"]
    em.emit("v", lambda e: e.tensor_tensor(
        mxA, buf[:, 0:ntok], buf[:, ntok:2 * ntok], ALU.max))
    em.emit("v", lambda e: e.tensor_tensor(
        mnA, buf[:, 0:ntok], buf[:, ntok:2 * ntok], ALU.min))
    srcx, dstx = mxA, mxB
    srcn, dstn = mnA, mnB
    for c in range(2, nchunks):
        em.emit("v", lambda e, c=c, s=srcx, d=dstx: e.tensor_tensor(
            d, s, buf[:, c * ntok:(c + 1) * ntok], ALU.max))
        em.emit("v", lambda e, c=c, s=srcn, d=dstn: e.tensor_tensor(
            d, s, buf[:, c * ntok:(c + 1) * ntok], ALU.min))
        srcx, dstx = dstx, srcx
        srcn, dstn = dstn, srcn
    vsp(b)
    em.emit("v", lambda e, s=srcn, d=dstn: e.tensor_scalar(
        d, s, -1.0, None, ALU.mult))
    vsp(b)
    em.emit("v", lambda e, sx=srcx, dn=dstn, d=dstx: e.tensor_tensor(
        d, sx, dn, ALU.max), inc=("v", 1))
    acc = dstx
    vtree = em.val("v")
    ntile = ntok // P
    mxc = b.sb["tmpf"][:, 16:16 + ntile]
    for blk in range(ntile):
        tt = em.emit("t", lambda e, blk=blk: e.transpose(
            b.sb["pT"][:, 0:P], acc[:, blk * P:(blk + 1) * P], ident),
            waits=[("v", vtree if blk == 0 else em.val("v"))], inc=("t", 1))
        em.emit("v", lambda e, blk=blk: e.reduce_max(
            mxc[:, blk:blk + 1], b.sb["pT"][:, 0:P], axis=AX.X,
            apply_absolute_value=True), waits=[("t", tt)], inc=("v", 1))
    vmx = em.val("v")
    mxb = b.sb["lntm"][:, 0:ntile]
    sc1 = em.emit("s", lambda e: e.copy(mxb, mxc), waits=[("v", vmx)],
                  inc=("s", 1))
    tt2 = em.emit("t", lambda e: e.transpose(
        b.sb["pT"][0:ntile, 0:P], mxb, ident), waits=[("s", sc1)], inc=("t", 1))
    sc2 = em.emit("s", lambda e: e.copy(b.sb["tmpf"][0:ntile, 32:32 + P],
                                        b.sb["pT"][0:ntile, 0:P]),
                  waits=[("t", tt2)], inc=("s", 1))
    em.dma("g", rows[0:1, R_MX:R_MX + ntok], b.sb["tmpf"][0:ntile, 32:32 + P],
           waits=[("s", sc2)])
    gdr = em.val("gd")
    r_my = rows[0:1, R_MY:R_MY + ntok]
    r_sy = rows[0:1, R_SY:R_SY + ntok]
    mxrow = rows[0:1, R_MX:R_MX + ntok]
    r_t = rows[0:1, R_SCTMP:R_SCTMP + ntok]
    r_alpha = b.sb["rowsb"][0:1, alpha_off:alpha_off + ntok]
    r_sc = b.sb["rowsb"][0:1, RB_SC:RB_SC + ntok]
    if prerow is not None:
        em.emit("v", lambda e: e.tensor_tensor(r_t, mxrow, prerow, ALU.mult),
                waits=[("gd", gdr)])
        vsp(b)
        em.emit("v", lambda e: e.tensor_scalar(r_my, r_t, rsn, 1e-6,
                                               ALU.mult, ALU.max))
    else:
        em.emit("v", lambda e: e.tensor_scalar(r_my, mxrow, rsn, 1e-6,
                                               ALU.mult, ALU.max),
                waits=[("gd", gdr)])
    vsp(b)
    em.emit("v", lambda e: e.tensor_scalar(
        r_alpha, r_my, wscale_ap, 1.0 / 127.0, ALU.mult, ALU.mult))
    em.emit("v", lambda e: e.reciprocal(r_sy, r_my))
    vsp(b)
    if prerow is not None:
        em.emit("v", lambda e: e.tensor_tensor(r_t, r_sy, prerow, ALU.mult))
        vsp(b)
        em.emit("v", lambda e: e.tensor_scalar(
            r_sc, r_t, 127.0 * rsn, None, ALU.mult), inc=("v", 1))
    else:
        em.emit("v", lambda e: e.tensor_scalar(
            r_sc, r_sy, 127.0 * rsn, None, ALU.mult), inc=("v", 1))
    vrow = em.val("v")

    for i in range((ntok + 511) // 512):
        w = min(512, ntok - i * 512)
        tmm = em.emit("t", lambda e, i=i, w=w: e.matmul(
            b.sb["pA"][:, 0:w], b.sb["onescol"],
            r_sc[0:1, i * 512:i * 512 + w], start=True, stop=True),
            waits=[("v", vrow)] if i == 0 else [("s", em.val("s"))],
            inc=("t", 1))
        em.emit("s", lambda e, i=i, w=w: e.copy(
            b.sb["srep"][:, i * 512:i * 512 + w], b.sb["pA"][:, 0:w]),
            waits=[("t", tmm)], inc=("s", 1))
    srdy = em.val("s")
    tmpq = b.sb["tmpf"][:, 0:ntok]
    hw = ntok // 2
    for c in range(nchunks):
        c0 = c * ntok
        em.emit("v", lambda e, c0=c0: e.tensor_tensor(
            tmpq[:, 0:hw], buf[:, c0:c0 + hw], b.sb["srep"][:, 0:hw],
            ALU.mult), waits=[("s", srdy)] if c == 0 else [])
        em.emit("v", lambda e, c0=c0: e.tensor_tensor(
            tmpq[:, hw:ntok], buf[:, c0 + hw:c0 + ntok],
            b.sb["srep"][:, hw:ntok], ALU.mult))
        em.emit("v", lambda e, c0=c0: e.tensor_scalar(
            buf[:, c0:c0 + hw], tmpq[:, 0:hw], RND, -RND, ALU.add, ALU.add))
        em.emit("v", lambda e, c0=c0: e.tensor_scalar(
            buf[:, c0 + hw:c0 + ntok], tmpq[:, hw:ntok], RND, -RND,
            ALU.add, ALU.add), inc=("v", 1) if c == nchunks - 1 else None)
    return em.val("v"), r_alpha


def dbg_tap(b, phase, panel, buf, ncols):
    em = b.em
    if DBG_TAP != (phase, panel):
        return
    vw = em.val("v"); sw = em.val("s"); tw = em.val("t")
    for i in range((ncols + 1023) // 1024):
        w = min(1024, ncols - i * 1024)
        sc = em.emit("s", lambda e, i=i, w=w: e.copy(
            b.sb["big2"].bitcast(F32)[:, 6144:6144 + w] if False else
            b.sb["tmpf"][:, 0:w], buf[:, i * 1024:i * 1024 + w]),
            waits=[("v", vw), ("t", tw), ("yd", em.val("yd"))], inc=("s", 1))
        em.dma("y", b.io["dbg"][:, i * 1024:i * 1024 + w], b.sb["tmpf"][:, 0:w],
               waits=[("s", sc)])


def l0ffn_call(b, prep, panel_tok0_list, x_add_scale):
    em = b.em
    x = b.sb["x"]
    big1 = b.sb["big1"]
    big2 = b.sb["big2"]
    h128 = b.sb["h128"]
    wt = b.sb["wti"]
    ntokP = PANEL
    ntile = ntokP // P
    apply_hist = []
    for tok0 in panel_tok0_list:
        lnf = big2[:, 0:DC * ntokP]
        for j in range(ntile):
            xt = x[:, (tok0 // P + j) * D:(tok0 // P + j + 1) * D]
            wait_ap = apply_hist[-1] if len(apply_hist) >= 1 else 0
            ap = ln_tile(b, xt, j % 2, wait_ap)
            apply_hist.append(ap)
            transpose_tile_to_fm(b, lnf, ntokP, j, ap)
        s_lnf = em.val("s")
        dbg_tap(b, "lnf", tok0, lnf, DC * ntokP)
        if FFN_MAXPHASE < 2:
            continue
        for c in range(DC):
            for hh in range(2):
                idx = c * 2 + hh
                pb = b.sb["pA"] if idx % 2 == 0 else b.sb["pB"]
                tmm = em.emit("t", lambda e, c=c, hh=hh, pb=pb: e.matmul(
                    pb[:, 0:512],
                    h128, lnf[:, c * ntokP + hh * 512: c * ntokP + hh * 512 + 512],
                    start=True, stop=True),
                    waits=[("s", s_lnf if idx < 2 else em.val("s"))],
                    inc=("t", 1))
                em.emit("s", lambda e, c=c, hh=hh, pb=pb: e.copy(
                    big1[:, c * ntokP + hh * 512: c * ntokP + hh * 512 + 512],
                    pb[:, 0:512]), waits=[("t", tmm)], inc=("s", 1))
        s_u1 = em.val("s")
        dbg_tap(b, "u1", tok0, big1, DC * ntokP)
        if FFN_MAXPHASE < 3:
            continue
        em.emit("v", None, waits=[("s", s_u1)])
        fwht_chunks(b, big1, DC, ntokP)
        dbg_tap(b, "u1f", tok0, big1, DC * ntokP)
        if FFN_MAXPHASE < 4:
            continue
        vq1, r_alpha1 = quant_fm(b, big1, DC, ntokP, RB_ALPHA, RS2K,
                                 b.sb["scl"][0:1, 0:1], None)
        if FFN_MAXPHASE < 5:
            continue
        dbg_tap(b, "q1", tok0, big1, DC * ntokP)
        tup3 = b.io["tupS"].rearrange("(k p) o -> p k o", p=P)
        for oc in range(32):
            wl = [("t", em.val("t"))] if oc >= 2 else \
                 [("gd", prep["gd_spill"]), ("v", vq1)]
            em.dma("g", wt[:, (oc % 2) * 2048:(oc % 2) * 2048 + 2048]
                   .rearrange("p (k o) -> p k o", o=128),
                   tup3[:, :, oc * 128:(oc + 1) * 128], waits=wl)
            ldw = em.val("gd")
            base = (oc % 2) * 2048
            pb = b.sb["pC"] if oc % 2 == 0 else b.sb["pD"]

            def mm(e, base=base, pb=pb):
                inst = None
                for k in range(DC):
                    for hh in range(2):
                        inst = e.matmul(
                            pb[:, hh * 512:hh * 512 + 512],
                            wt[:, base + k * 128: base + k * 128 + 128],
                            big1[:, k * ntokP + hh * 512: k * ntokP + hh * 512 + 512],
                            start=(k == 0), stop=(k == DC - 1))
                return inst
            wl = [("gd", ldw), ("v", vq1)]
            if oc >= 2:
                wl.append(("s", em.val("s")))
            tmm = em.emit("t", mm, waits=wl, inc=("t", 1))
            sr = em.emit("s", lambda e, pb=pb: e.activation(
                b.sb["dscb"][:, 0:ntokP], pb[:, 0:ntokP], AF.Relu,
                bias=0.0, scale=1.0), waits=[("t", tmm)], inc=("s", 1))
            vh = em.emit("v", lambda e, pb=pb: e.tensor_tensor(
                b.sb["junkb"][:, 0:ntokP], b.sb["dscb"][:, 0:ntokP],
                pb[:, 0:ntokP], ALU.mult), waits=[("s", sr)], inc=("v", 1))

            def mm2(e):
                e.matmul(b.sb["pA"][:, 0:512], h128,
                         b.sb["junkb"][:, 0:512], start=True, stop=True)
                return e.matmul(b.sb["pB"][:, 0:512], h128,
                                b.sb["junkb"][:, 512:1024], start=True, stop=True)
            wl2 = [("v", vh), ("s", em.val("s"))]
            tm2 = em.emit("t", mm2, waits=wl2, inc=("t", 1))

            def ev2(e, oc=oc):
                e.copy(big2[:, oc * ntokP:oc * ntokP + 512], b.sb["pA"][:, 0:512])
                return e.copy(big2[:, oc * ntokP + 512:oc * ntokP + 1024],
                              b.sb["pB"][:, 0:512])
            em.emit("s", ev2, waits=[("t", tm2)], inc=("s", 1))
        s_u2 = em.val("s")
        dbg_tap(b, "u2", tok0, big2, 32 * ntokP)
        if FFN_MAXPHASE < 6:
            continue
        em.emit("v", None, waits=[("s", s_u2)])
        fwht_chunks(b, big2, 32, ntokP)
        dbg_tap(b, "u2f", tok0, big2, 32 * ntokP)
        if FFN_MAXPHASE < 7:
            continue
        r_a1sq = b.sb["rowsb"][0:1, RB_A1SQ:RB_A1SQ + ntokP]
        em.emit("v", lambda e: e.tensor_tensor(r_a1sq, r_alpha1, r_alpha1,
                                               ALU.mult))
        vq2, r_alpha2 = quant_fm(b, big2, 32, ntokP, RB_ALPHA, RS4K,
                                 b.sb["scl"][0:1, 1:2], r_a1sq)
        r_beta = b.sb["rowsb"][0:1, RB_BETA:RB_BETA + ntokP]
        em.emit("v", lambda e: e.tensor_scalar(
            r_beta, r_alpha2, x_add_scale, None, ALU.mult), inc=("v", 1))
        vb = em.val("v")

        for i in range(ntokP // 512):
            tmb = em.emit("t", lambda e, i=i: e.matmul(
                b.sb["pA"][:, 0:512], b.sb["onescol"],
                r_beta[0:1, i * 512:(i + 1) * 512], start=True, stop=True),
                waits=[("v", vb)] if i == 0 else [("s", em.val("s"))],
                inc=("t", 1))
            em.emit("s", lambda e, i=i: e.copy(
                b.sb["srep"][:, i * 512:(i + 1) * 512], b.sb["pA"][:, 0:512]),
                waits=[("t", tmb)], inc=("s", 1))
        s_brep = em.val("s")
        dbg_tap(b, "q2", tok0, big2, 32 * ntokP)
        if FFN_MAXPHASE < 8:
            continue
        tdn3 = b.io["tdownS"].rearrange("(k p) o -> p k o", p=P)
        wtH = big1
        for f in range(DC):
            wl = [("t", em.val("t"))] if f >= 2 else [("v", vq2)]
            em.dma("g", wtH[:, (f % 2) * 4096:(f % 2) * 4096 + 4096]
                   .rearrange("p (k o) -> p k o", o=128),
                   tdn3[:, :, f * 128:(f + 1) * 128], waits=wl)
            ldw = em.val("gd")
            pb = b.sb["pC"] if f % 2 == 0 else b.sb["pD"]

            def mm(e, f=f, pb=pb):
                inst = None
                for k in range(32):
                    for hh in range(2):
                        inst = e.matmul(
                            pb[:, hh * 512:hh * 512 + 512],
                            wtH[:, (f % 2) * 4096 + k * 128:
                                (f % 2) * 4096 + k * 128 + 128],
                            big2[:, k * ntokP + hh * 512:k * ntokP + hh * 512 + 512],
                            start=(k == 0), stop=(k == 31))
                return inst
            wl = [("gd", ldw), ("v", vq2), ("s", s_brep)]
            if f >= 2:
                wl.append(("v", em.val("v")))
            tmm = em.emit("t", mm, waits=wl, inc=("t", 1))
            vd = em.emit("v", lambda e, pb=pb: e.tensor_tensor(
                b.sb["dscb"][:, 0:ntokP], pb[:, 0:ntokP],
                b.sb["srep"][:, 0:ntokP], ALU.mult),
                waits=[("t", tmm)], inc=("v", 1))
            for g in range(ntile // 4):
                def tr(e, g=g):
                    inst = None
                    for k in range(4):
                        blk = g * 4 + k
                        inst = e.transpose(
                            b.sb["pT"][:, k * P:(k + 1) * P],
                            b.sb["dscb"][:, blk * P:(blk + 1) * P],
                            b.sb["ident"])
                    return inst
                tt = em.emit("t", tr, waits=[("v", em.val("v"))], inc=("t", 1))

                def add(e, g=g, f=f, tok0=tok0):
                    jb = b.sb["junkb"]
                    for k in range(4):
                        blk = g * 4 + k
                        jt = tok0 // P + blk
                        e.scalar_tensor_tensor(
                            jb[:, k * P:(k + 1) * P],
                            b.sb["pT"][:, k * P:(k + 1) * P], 1.0,
                            x[:, jt * D + f * P: jt * D + (f + 1) * P],
                            ALU.mult, ALU.add)
                    inst = None
                    for k in range(4):
                        blk = g * 4 + k
                        jt = tok0 // P + blk
                        inst = e.tensor_copy(
                            x[:, jt * D + f * P: jt * D + (f + 1) * P],
                            jb[:, k * P:(k + 1) * P])
                    return inst
                em.emit("v", add, waits=[("t", tt)], inc=("v", 1))



def l0attn_call(b):
    em = b.em
    x = b.sb["x"]
    big1 = b.sb["big1"]
    big2 = b.sb["big2"]
    lntm = b.sb["lntm"]
    tmpf = b.sb["tmpf"]
    ddt = b.sb["ddt"]
    kdc = b.sb["kdc"]
    kvf = b.sb["kvf"]
    ssq = b.sb["ssq"]
    psA, psB, psC, pT = b.sb["pA"], b.sb["pB"], b.sb["pC"], b.sb["pT"]
    ident = b.sb["ident"]
    SCT, QDQ, KTK, VTK, KVB, OEV = 0, 512, 768, 1024, 1280, 1408

    em.dma("y", b.sb["slprow"], b.io["slope"])
    em.emit("v", lambda e: e.memset(b.sb["rowsf"][0:1, 0:P], 1.0),
            waits=[("v", em.val("v"))], inc=("v", 1))
    mm = em.emit("t", lambda e: e.matmul(psA[:, 0:16], b.sb["rowsf"][0:1, 0:P],
                                         b.sb["slprow"], start=True, stop=True),
                 waits=[("yd", em.val("yd")), ("v", em.val("v")),
                        ("s", em.val("s"))], inc=("t", 1))
    sb1 = em.emit("s", lambda e: e.copy(tmpf[:, 0:16], psA[:, 0:16]),
                  waits=[("t", mm)], inc=("s", 1))
    em.emit("v", lambda e: e.tensor_scalar(b.sb["slpn"], tmpf[:, 0:16],
                                           -1.0, None, ALU.mult),
            waits=[("s", sb1)], inc=("v", 1))
    vslp = em.val("v")
    sbd = em.emit("s", lambda e: e.activation(b.sb["bdec"], tmpf[:, 0:16],
                                              AF.Exp, bias=0.0, scale=-256.0),
                  waits=[("v", vslp)], inc=("s", 1))
    em.emit("v", lambda e: e.memset(ssq[:, 0:16], 0.0))

    if ATTN_MAXPHASE < 2:
        return
    lnf = big2[:, 0:16 * 2048]
    ap = em.val("s")
    for j in range(NT):
        xt = x[:, j * D:(j + 1) * D]
        ap = ln_tile(b, xt, j % 2, ap, attn=True)
        transpose_tile_to_fm(b, lnf, 2048, j, ap)
    s_fm = em.val("s")

    if ATTN_MAXPHASE < 3:
        return
    wq3 = b.io["wqkvuT"].rearrange("(k p) o -> p k o", p=P)
    slpn, bdec = b.sb["slpn"], b.sb["bdec"]
    pend = None
    for h in range(ATTN_NHEADS):
        for i, off in enumerate((0, D, 2 * D, 3 * D)):
            c0 = off + h * P
            em.dma("g", big1[:, i * 2048:(i + 1) * 2048]
                   .rearrange("p (k o) -> p k o", o=P),
                   wq3[:, :, c0:c0 + P],
                   waits=[("t", em.val("t"))] if i == 0 else [])
        gw = em.val("gd")
        sdd = em.emit("s", lambda e, h=h: e.activation(
            ddt[:, 0:256], b.sb["tabT0"], AF.Exp, bias=0.0,
            scale=slpn[:, h:h + 1]),
            waits=[("v", em.val("v")), ("t", em.val("t"))], inc=("s", 1))
        em.emit("s", lambda e, h=h: e.activation(
            ddt[:, 256:512], b.sb["tabT1"], AF.Exp, bias=0.0,
            scale=slpn[:, h:h + 1]))
        em.emit("s", lambda e, h=h: e.activation(
            lntm[:, 1536:1792], b.sb["arr1"], AF.Exp, bias=0.0,
            scale=slpn[:, h:h + 1]))
        em.emit("s", lambda e, h=h: e.activation(
            kdc, b.sb["karrc"], AF.Exp, bias=0.0,
            scale=slpn[:, h:h + 1]), inc=("s", 1))
        s_dd = em.val("s")
        qf = big1[:, 8192:10240]
        kf = big1[:, 10240:12288]
        vf = big1[:, 12288:14336]
        ofm = big1[:, 14336:16384]
        for pi, (wbase, dst, act) in enumerate((
                (0, qf, AF.Silu), (2048, kf, AF.Silu), (4096, vf, AF.Copy))):
            for tg in range(4):
                def mmp(e, wbase=wbase, tg=tg):
                    inst = None
                    for k in range(16):
                        inst = e.matmul(
                            psA,
                            big1[:, wbase + k * P:wbase + (k + 1) * P],
                            lnf[:, k * 2048 + tg * 512:k * 2048 + tg * 512 + 512],
                            start=(k == 0), stop=(k == 15))
                    return inst
                wl = [("s", em.val("s"))]
                if pi == 0 and tg == 0:
                    wl += [("gd", gw), ("s", s_fm), ("yd", em.val("yd"))]
                tm = em.emit("t", mmp, waits=wl, inc=("t", 1))
                if act is AF.Copy:
                    em.emit("s", lambda e, dst=dst, tg=tg: e.copy(
                        dst[:, tg * 512:(tg + 1) * 512], psA),
                        waits=[("t", tm)], inc=("s", 1))
                else:
                    em.emit("s", lambda e, dst=dst, tg=tg, act=act: e.activation(
                        dst[:, tg * 512:(tg + 1) * 512], psA, act,
                        bias=0.0, scale=1.0), waits=[("t", tm)], inc=("s", 1))
        s_proj = em.val("s")
        if ATTN_MAXPHASE < 4:
            continue
        em.emit("v", lambda e: e.memset(kvf[:, :], 0.0),
                waits=[("s", s_proj)])
        em.emit("v", lambda e: e.memset(lntm[:, KVB:KVB + P], 0.0),
                inc=("v", 1))
        kvb_s = em.val("s")
        for blk in range(ATTN_NBLK):
            t0 = blk * 256
            def mms(e, t0=t0):
                for sh in range(2):
                    e.matmul(psA[:, sh * 256:sh * 256 + 256],
                             kf[:, t0 + sh * P:t0 + (sh + 1) * P],
                             qf[:, t0:t0 + 256], start=True, stop=True)
                return e.transpose(pT[:, 0:P], kf[:, t0:t0 + P], ident)
            wl = [("s", s_proj if blk == 0 else em.val("s")),
                  ("v", em.val("v"))]
            tsc = em.emit("t", mms, waits=wl, inc=("t", 1))
            if ATTN_SCANPART == 11:
                continue

            def mmt(e, t0=t0):
                e.transpose(pT[:, P:2 * P], kf[:, t0 + P:t0 + 2 * P], ident)
                e.transpose(pT[:, 2 * P:3 * P], vf[:, t0:t0 + P], ident)
                return e.transpose(pT[:, 3 * P:4 * P],
                                   vf[:, t0 + P:t0 + 2 * P], ident)
            ttr = em.emit("t", mmt, inc=("t", 1))
            if ATTN_SCANPART == 12:
                continue
            for sh in range(2):
                em.emit("v", lambda e, sh=sh: e.tensor_tensor(
                    lntm[:, SCT + sh * 256:SCT + sh * 256 + 256],
                    psA[:, sh * 256:sh * 256 + 256],
                    ddt[:, sh * 256:sh * 256 + 256], ALU.mult),
                    waits=[("t", tsc), ("s", s_dd)] if sh == 0 else [],
                    inc=("v", 1) if (sh == 1 and ATTN_SCANPART == 13) else None)
                if ATTN_SCANPART == 13:
                    continue
                em.emit("v", lambda e, sh=sh: e.tensor_scalar(
                    lntm[:, KTK + sh * P:KTK + (sh + 1) * P],
                    pT[:, sh * P:(sh + 1) * P], kdc[:, sh:sh + 1], None,
                    ALU.mult), waits=[("t", ttr)] if sh == 0 else [],
                    inc=("v", 1) if sh == 1 else None)
            if ATTN_SCANPART == 13:
                continue
            if ATTN_SCANPART == 14:
                continue
            def vte(e):
                e.copy(lntm[:, VTK:VTK + P], pT[:, 2 * P:3 * P])
                return e.copy(lntm[:, VTK + P:VTK + 256], pT[:, 3 * P:4 * P])
            svt = em.emit("s", vte, waits=[("t", ttr), ("v", em.val("v"))],
                          inc=("s", 1))
            if ATTN_SCANPART == 15:
                continue
            for mh in range(2):
                em.emit("v", lambda e, t0=t0, mh=mh: e.tensor_tensor(
                    lntm[:, QDQ + mh * P:QDQ + (mh + 1) * P],
                    qf[:, t0 + mh * P:t0 + (mh + 1) * P],
                    lntm[:, 1536 + mh * P:1536 + (mh + 1) * P], ALU.mult),
                    inc=("v", 1) if mh == 1 else None)
            v_blk = em.val("v")
            if ATTN_SCANPART < 2:
                continue
            for mh in range(2):
                def mmo(e, mh=mh, blk=blk):
                    first = True
                    if blk > 0:
                        e.matmul(psB[:, 0:P], lntm[:, QDQ + mh * P:QDQ + (mh + 1) * P],
                                 lntm[:, KVB:KVB + P], start=True, stop=False)
                        first = False
                    for sh in range(2):
                        inst = e.matmul(
                            psB[:, 0:P],
                            lntm[:, SCT + sh * 256 + mh * P:SCT + sh * 256 + (mh + 1) * P],
                            lntm[:, VTK + sh * P:VTK + (sh + 1) * P],
                            start=first and sh == 0, stop=(sh == 1))
                    return inst
                tmo = em.emit("t", mmo,
                              waits=[("v", v_blk), ("s", em.val("s")),
                                     ("s", kvb_s)], inc=("t", 1))
                jt = blk * 2 + mh
                sev = em.emit("s", lambda e: e.copy(
                    lntm[:, OEV:OEV + P], psB[:, 0:P]),
                    waits=[("t", tmo)], inc=("s", 1))
                ssv = em.emit("s", lambda e: e.activation(
                    tmpf[:, 128:256], psB[:, 0:P], AF.Square,
                    bias=0.0, scale=1.0), waits=[("v", em.val("v"))],
                    inc=("s", 1))
                nonlocal_pend = pend
                em.emit("v", lambda e, jt=jt: e.reduce_sum(
                    tmpf[:, jt % 2:jt % 2 + 1], tmpf[:, 128:256], axis=AX.X),
                    waits=[("s", ssv)], inc=("v", 1))
                rv = em.val("v")
                if nonlocal_pend is not None:
                    pj, ps_ = nonlocal_pend
                    em.emit("v", lambda e, pj=pj, ps_=ps_: e.tensor_add(
                        ssq[:, pj:pj + 1], ssq[:, pj:pj + 1],
                        tmpf[:, ps_:ps_ + 1]))
                pend = (jt, jt % 2)
                if ATTN_SCANPART < 3:
                    continue
                tot = em.emit("t", lambda e, mh=mh: e.transpose(
                    pT[:, 0:P] if mh == 0 else pT[:, P:2 * P],
                    lntm[:, OEV:OEV + P], ident),
                    waits=[("s", sev), ("v", em.val("v"))], inc=("t", 1))
                em.emit("s", lambda e, t0=t0, mh=mh: e.copy(
                    ofm[:, t0 + mh * P:t0 + (mh + 1) * P],
                    pT[:, 0:P] if mh == 0 else pT[:, P:2 * P]),
                    waits=[("t", tot)], inc=("s", 1))
            if ATTN_SCANPART < 4:
                continue
            def mmk(e):
                for sh in range(2):
                    inst = e.matmul(psC[:, 0:P],
                                    lntm[:, KTK + sh * P:KTK + (sh + 1) * P],
                                    lntm[:, VTK + sh * P:VTK + (sh + 1) * P],
                                    start=(sh == 0), stop=(sh == 1))
                return inst
            tkv = em.emit("t", mmk, waits=[("v", v_blk), ("s", svt)],
                          inc=("t", 1))
            em.emit("v", lambda e, h=h: e.scalar_tensor_tensor(
                kvf[:, :], kvf[:, :], bdec[:, h:h + 1], psC[:, 0:P],
                ALU.mult, ALU.add), waits=[("t", tkv), ("s", sbd)],
                inc=("v", 1))
            kvb_s = em.emit("s", lambda e: e.copy(
                lntm[:, KVB:KVB + P], kvf[:, :]),
                waits=[("v", em.val("v"))], inc=("s", 1))
        if ATTN_MAXPHASE < 5:
            continue
        pf = qf
        for tg in range(4):
            def mmu(e, tg=tg):
                inst = None
                for k in range(16):
                    inst = e.matmul(
                        psA,
                        big1[:, 6144 + k * P:6144 + (k + 1) * P],
                        lnf[:, k * 2048 + tg * 512:k * 2048 + tg * 512 + 512],
                        start=(k == 0), stop=(k == 15))
                return inst
            tmu = em.emit("t", mmu, waits=[("v", em.val("v")),
                                           ("s", em.val("s"))], inc=("t", 1))
            em.emit("v", lambda e, tg=tg: e.tensor_tensor(
                pf[:, tg * 512:(tg + 1) * 512], psA,
                ofm[:, tg * 512:(tg + 1) * 512], ALU.mult),
                waits=[("t", tmu), ("s", em.val("s"))], inc=("v", 1))
        em.dma("y", b.io["ptmp"][:, h * 2048:(h + 1) * 2048],
               pf[:, 0:2048], waits=[("v", em.val("v"))])
    if ATTN_MAXPHASE < 6 or pend is None:
        return
    vsp(b)
    pj, ps_ = pend
    em.emit("v", lambda e, pj=pj, ps_=ps_: e.tensor_add(
        ssq[:, pj:pj + 1], ssq[:, pj:pj + 1], tmpf[:, ps_:ps_ + 1]),
        inc=("v", 1))
    v_ssq = em.val("v")
    s_rr = em.emit("s", lambda e: e.activation(
        ssq[:, 16:32], ssq[:, 0:16], AF.Sqrt, bias=1e-6, scale=1.0 / D),
        waits=[("v", v_ssq)], inc=("s", 1))
    em.emit("v", lambda e: e.reciprocal(ssq[:, 32:48], ssq[:, 16:32]),
            waits=[("s", s_rr)], inc=("v", 1))
    v_rr = em.val("v")
    wo3 = b.io["woutT"].rearrange("(k p) o -> p k o", p=P)
    for k in range(16):
        em.dma("g", big2[:, k * 2048:(k + 1) * 2048],
               wo3[:, k, :], waits=[("t", em.val("t"))] if k == 0 else [])
    g_wo = em.val("gd")
    pt3 = b.io["ptmp"].rearrange("p (h t) -> p h t", t=2048)
    for tg in range(4):
        em.dma("y", big1[:, 0:8192].rearrange("p (h t) -> p h t", t=512),
               pt3[:, :, tg * 512:(tg + 1) * 512],
               waits=[("t", em.val("t"))])
        y_p = em.val("yd")
        for osl in range(16):
            def mmo2(e, tg=tg, osl=osl):
                inst = None
                for k in range(16):
                    inst = e.matmul(
                        psA,
                        big2[:, k * 2048 + osl * P:k * 2048 + (osl + 1) * P],
                        big1[:, k * 512:(k + 1) * 512],
                        start=(k == 0), stop=(k == 15))
                return inst
            wl = [("s", em.val("s"))]
            if osl == 0:
                wl += [("yd", y_p), ("v", v_rr)]
                if tg == 0:
                    wl.append(("gd", g_wo))
            tmo2 = em.emit("t", mmo2, waits=wl, inc=("t", 1))
            sev2 = em.emit("s", lambda e: e.copy(lntm[:, 0:512], psA),
                           waits=[("t", tmo2)], inc=("s", 1))
            tt2 = em.emit("t", lambda e: [
                e.transpose(pT[:, jj * P:(jj + 1) * P],
                            lntm[:, jj * P:(jj + 1) * P], ident)
                for jj in range(4)][-1],
                waits=[("s", sev2), ("v", em.val("v"))], inc=("t", 1))

            def upd(e, tg=tg, osl=osl):
                jb = b.sb["junkb"]
                for jj in range(4):
                    jt = tg * 4 + jj
                    e.scalar_tensor_tensor(
                        jb[:, jj * P:(jj + 1) * P],
                        pT[:, jj * P:(jj + 1) * P],
                        ssq[:, 32 + jt:33 + jt],
                        x[:, jt * D + osl * P:jt * D + (osl + 1) * P],
                        ALU.mult, ALU.add)
                inst = None
                for jj in range(4):
                    jt = tg * 4 + jj
                    inst = e.tensor_copy(
                        x[:, jt * D + osl * P:jt * D + (osl + 1) * P],
                        jb[:, jj * P:(jj + 1) * P])
                return inst
            em.emit("v", upd, waits=[("t", tt2)], inc=("v", 1))



def l1ffn_call(b, x_scale):
    em = b.em
    x = b.sb["x"]
    big1 = b.sb["big1"]
    big2 = b.sb["big2"]
    wti = b.sb["wti"]
    l1s = b.sb["l1s"]
    sdrep = b.sb["sdrep"]
    dscb = b.sb["dscb"]
    jb = b.sb["junkb"]
    pA, pB, pC, pD = b.sb["pA"], b.sb["pB"], b.sb["pC"], b.sb["pD"]
    em.dma("y", l1s[:, 0:64], b.io["l1sg"])
    em.dma("y", l1s[:, 64:128], b.io["l1su"])
    em.dma("y", b.sb["rowsf"][0:1, 0:D], b.io["l1sd"],
           waits=[("v", em.val("v")), ("t", em.val("t")),
                  ("s", em.val("s"))])
    yd_s = em.val("yd")
    sc0 = em.emit("s", lambda e: e.copy(b.sb["rowsb"][0:1, 0:D],
                                        b.sb["rowsf"][0:1, 0:D]),
                  waits=[("yd", yd_s), ("v", em.val("v"))], inc=("s", 1))
    for og in range(4):
        mm = em.emit("t", lambda e, og=og: e.matmul(
            pA, b.sb["onescol"],
            b.sb["rowsb"][0:1, og * 512:(og + 1) * 512], start=True, stop=True),
            waits=[("s", sc0), ("v", em.val("v"))] if og == 0 else
                  [("s", em.val("s"))],
            inc=("t", 1))
        em.emit("s", lambda e, og=og: e.activation(
            sdrep[:, og * 512:(og + 1) * 512], pA, AF.Identity,
            bias=0.0, scale=float(x_scale)), waits=[("t", mm)], inc=("s", 1))
    s_prep = em.val("s")
    dbg_tap(b, "l1sd", 0, sdrep, D)

    wg3 = b.io["l1wg"].rearrange("(k p) i -> p k i", p=P)
    wu3 = b.io["l1wu"].rearrange("(k p) i -> p k i", p=P)

    def hh_ap(c):
        if c < 48:
            return big2[:, 8192 + c * 512:8192 + (c + 1) * 512]
        return big1[:, (c - 48) * 512:(c - 47) * 512]

    apply_hist = []
    for g in range(4):
        lnf = big2[:, 0:8192]
        for jl in range(4):
            jt = g * 4 + jl
            xt = x[:, jt * D:(jt + 1) * D]
            wait_ap = apply_hist[-1] if apply_hist else em.val("s")
            ap = ln_tile(b, xt, jl % 2, wait_ap)
            apply_hist.append(ap)
            transpose_tile_to_fm(b, lnf, 512, jl, ap)
        s_lnf = em.val("s")
        slots = (wti[:, 0:4096], big1[:, 8192:12288], big1[:, 12288:16384])
        for isl in range(32):
            sa = slots[(2 * isl) % 3]
            sb_ = slots[(2 * isl + 1) % 3]
            em.dma("g", sa.rearrange("p (k i) -> p k i", i=256),
                   wg3[:, :, isl * 256:(isl + 1) * 256],
                   waits=[("t", em.val("t"))])
            em.dma("g", sb_.rearrange("p (k i) -> p k i", i=256),
                   wu3[:, :, isl * 256:(isl + 1) * 256])
            gwd = em.val("gd")
            for icl in range(2):
                ic = isl * 2 + icl
                def mmg(e, sa=sa, icl=icl):
                    inst = None
                    for k in range(16):
                        inst = e.matmul(
                            pA, sa[:, k * 256 + icl * P:k * 256 + (icl + 1) * P],
                            lnf[:, k * 512:(k + 1) * 512],
                            start=(k == 0), stop=(k == 15))
                    return inst
                wl = [("gd", gwd), ("s", em.val("s"))]
                if isl == 0 and icl == 0:
                    wl.append(("s", s_lnf))
                    wl.append(("s", s_prep))
                tg_ = em.emit("t", mmg, waits=wl, inc=("t", 1))

                def mmw(e, sb_=sb_, icl=icl):
                    inst = None
                    for k in range(16):
                        inst = e.matmul(
                            pB, sb_[:, k * 256 + icl * P:k * 256 + (icl + 1) * P],
                            lnf[:, k * 512:(k + 1) * 512],
                            start=(k == 0), stop=(k == 15))
                    return inst
                tu_ = em.emit("t", mmw, waits=[("v", em.val("v"))],
                              inc=("t", 1))
                sg_ = em.emit("s", lambda e, ic=ic: e.activation(
                    dscb[:, 0:512], pA, AF.Silu, bias=0.0,
                    scale=l1s[:, ic:ic + 1]), waits=[("t", tg_)], inc=("s", 1))
                em.emit("v", lambda e, ic=ic: e.scalar_tensor_tensor(
                    hh_ap(ic), pB, l1s[:, 64 + ic:65 + ic], dscb[:, 0:512],
                    ALU.mult, ALU.mult), waits=[("t", tu_), ("s", sg_)],
                    inc=("v", 1))
        v_hh = em.val("v")
        if g == 0:
            dbg_tap(b, "l1hh", 0, big2[:, 8192:8192 + 8192], 8192)
            dbg_tap(b, "l1ln", 0, lnf, 8192)
        tslot = [em.val("t"), em.val("t")]
        for jl in range(4):
            jt = g * 4 + jl
            for ic in range(64):
                slot = ic % 2
                em.dma("g", big1[:, 8192 + slot * 2048:8192 + (slot + 1) * 2048],
                       b.io["l1wd"][ic * P:(ic + 1) * P, :],
                       waits=[("t", tslot[slot])])
                gdd = em.val("gd")

                def mmd(e, ic=ic, jl=jl, slot=slot):
                    inst = None
                    for og in range(4):
                        pb = pC if og < 2 else pD
                        inst = e.matmul(
                            pb[:, (og % 2) * 512:(og % 2 + 1) * 512],
                            hh_ap(ic)[:, jl * P:(jl + 1) * P],
                            big1[:, 8192 + slot * 2048 + (og % 4) * 512:
                                 8192 + slot * 2048 + (og % 4 + 1) * 512],
                            start=(ic == 0), stop=(ic == 63))
                    return inst
                wl = [("gd", gdd)]
                if ic == 0:
                    wl += [("v", v_hh if jl == 0 else em.val("v"))]
                tdn = em.emit("t", mmd, waits=wl, inc=("t", 1))
                tslot[slot] = tdn
            for pr in range(2):
                for og in (2 * pr, 2 * pr + 1):
                    pb = pC if og < 2 else pD
                    em.emit("v", lambda e, og=og, pb=pb: e.tensor_tensor(
                        jb[:, (og % 2) * 512:(og % 2 + 1) * 512],
                        pb[:, (og % 2) * 512:(og % 2 + 1) * 512],
                        sdrep[:, og * 512:(og + 1) * 512], ALU.mult),
                        waits=[("t", tdn)] if og == 2 * pr else [])
                for og in (2 * pr, 2 * pr + 1):
                    em.emit("v", lambda e, og=og, jt=jt: e.tensor_tensor(
                        dscb[:, (og % 2) * 512:(og % 2 + 1) * 512],
                        jb[:, (og % 2) * 512:(og % 2 + 1) * 512],
                        x[:, jt * D + og * 512:jt * D + (og + 1) * 512],
                        ALU.add))
                for og in (2 * pr, 2 * pr + 1):
                    em.emit("v", lambda e, og=og, jt=jt: e.tensor_copy(
                        x[:, jt * D + og * 512:jt * D + (og + 1) * 512],
                        dscb[:, (og % 2) * 512:(og % 2 + 1) * 512]),
                        inc=("v", 1) if og == 2 * pr + 1 else None)
            if g == 0 and jl == 0:
                dbg_tap(b, "l1x", 0, x[:, 0:D], D)



def l1attn_call(b):
    em = b.em
    x = b.sb["x"]
    big1 = b.sb["big1"]
    big2 = b.sb["big2"]
    lntm = b.sb["lntm"]
    wti = b.sb["wti"]
    l1s = b.sb["l1s"]
    borep = b.sb["sdrep"]
    ssq = b.sb["ssq"]
    dscb = b.sb["dscb"]
    srep = b.sb["srep"]
    jb = b.sb["junkb"]
    ident = b.sb["ident"]
    psA, psB, pC, pD, pT = (b.sb["pA"], b.sb["pB"], b.sb["pC"], b.sb["pD"],
                            b.sb["pT"])
    QB, KB, VT, CX = 8192, 10240, 12288, 14336

    def psc(c0, w):
        return pC[:, c0:c0 + w] if c0 < 1024 else pD[:, c0 - 1024:c0 - 1024 + w]

    em.dma("y", l1s[:, 0:16], b.io["l1bq"],
           waits=[("v", em.val("v")), ("s", em.val("s")), ("t", em.val("t"))])
    em.dma("y", b.sb["rowsf"][0:1, 0:D], b.io["l1bo"])
    yd0 = em.val("yd")
    sc0 = em.emit("s", lambda e: e.copy(b.sb["rowsb"][0:1, 0:D],
                                        b.sb["rowsf"][0:1, 0:D]),
                  waits=[("yd", yd0), ("v", em.val("v")),
                         ("t", em.val("t"))], inc=("s", 1))
    for og in range(4):
        mm = em.emit("t", lambda e, og=og: e.matmul(
            psA, b.sb["onescol"],
            b.sb["rowsb"][0:1, og * 512:(og + 1) * 512], start=True, stop=True),
            waits=[("s", sc0), ("v", em.val("v"))] if og == 0 else
                  [("s", em.val("s"))], inc=("t", 1))
        em.emit("s", lambda e, og=og: e.copy(
            borep[:, og * 512:(og + 1) * 512], psA), waits=[("t", mm)],
            inc=("s", 1))

    lnf = big2[:, 0:16 * 2048]
    ap = em.val("s")
    for j in range(NT):
        ap = ln_tile(b, x[:, j * D:(j + 1) * D], j % 2, ap, attn=True)
        transpose_tile_to_fm(b, lnf, 2048, j, ap)
    s_fm = em.val("s")

    wdn3 = b.io["l1wdn"].rearrange("(k p) l -> p k l", p=P)
    for lc in range(4):
        slot = lc % 2
        em.dma("g", wti[:, slot * 2048:slot * 2048 + 2048]
               .rearrange("p (k o) -> p k o", o=P),
               wdn3[:, :, lc * P:(lc + 1) * P], waits=[("t", em.val("t"))])
        gw = em.val("gd")
        for tg in range(4):
            mm = em.emit("t", lambda e, slot=slot, tg=tg: [
                e.matmul(psA, wti[:, slot * 2048 + k * P:slot * 2048 + (k + 1) * P],
                         lnf[:, k * 2048 + tg * 512:k * 2048 + tg * 512 + 512],
                         start=(k == 0), stop=(k == 15)) for k in range(16)][-1],
                waits=[("gd", gw), ("s", s_fm if lc == 0 and tg == 0
                                    else em.val("s"))], inc=("t", 1))
            em.emit("s", lambda e, lc=lc, tg=tg: e.copy(
                big1[:, lc * 2048 + tg * 512:lc * 2048 + tg * 512 + 512], psA),
                waits=[("t", mm)], inc=("s", 1))

    def rope(buf, cosio, s2io, ready_s):
        for tg in range(4):
            c0 = tg * 512
            dy = em.dma("y", lntm[:, 0:512], cosio[:, c0:c0 + 512],
                        waits=[("v", em.val("v")), ("s", em.val("s"))])
            em.dma("y", lntm[:, 512:1024], s2io[:, c0:c0 + 512])
            dy = em.val("yd")
            em.emit("v", lambda e, c0=c0: e.tensor_copy(
                lntm[0:64, 1024:1536], buf[64:128, c0:c0 + 512]),
                waits=[("s", ready_s), ("yd", dy)])
            em.emit("v", lambda e, c0=c0: e.tensor_copy(
                lntm[64:128, 1024:1536], buf[0:64, c0:c0 + 512]))
            em.emit("v", lambda e: e.tensor_tensor(
                dscb[:, 0:512], lntm[:, 1024:1536], lntm[:, 512:1024],
                ALU.mult))
            em.emit("v", lambda e, c0=c0: e.tensor_tensor(
                lntm[:, 1536:2048], buf[:, c0:c0 + 512], lntm[:, 0:512],
                ALU.mult))
            vsp(b)
            em.emit("v", lambda e, c0=c0: e.tensor_add(
                buf[:, c0:c0 + 512], lntm[:, 1536:2048], dscb[:, 0:512]),
                inc=("v", 1))
        return em.val("v")

    wq3 = b.io["l1wq"].rearrange("(k p) o -> p k o", p=P)
    wk3 = b.io["l1wk"].rearrange("(k p) o -> p k o", p=P)
    wv3 = b.io["l1wv"].rearrange("(k p) o -> p k o", p=P)
    for h in range(16):
        c0 = h * P
        em.dma("g", wti[:, 0:2048].rearrange("p (k o) -> p k o", o=P),
               wq3[:, :, c0:c0 + P], waits=[("t", em.val("t"))])
        em.dma("g", wti[:, 2048:2560].rearrange("p (k o) -> p k o", o=P),
               wk3[:, :, c0:c0 + P])
        em.dma("g", wti[:, 2560:3072].rearrange("p (k o) -> p k o", o=P),
               wv3[:, :, c0:c0 + P])
        gw = em.val("gd")
        for tg in range(4):
            mm = em.emit("t", lambda e, tg=tg: [
                e.matmul(psA, wti[:, k * P:(k + 1) * P],
                         lnf[:, k * 2048 + tg * 512:k * 2048 + tg * 512 + 512],
                         start=(k == 0), stop=(k == 15)) for k in range(16)][-1],
                waits=[("gd", gw), ("s", em.val("s")), ("v", em.val("v"))],
                inc=("t", 1))
            em.emit("s", lambda e, tg=tg, h=h: e.activation(
                big1[:, QB + tg * 512:QB + tg * 512 + 512], psA, AF.Identity,
                bias=l1s[:, h:h + 1], scale=1.0), waits=[("t", mm)],
                inc=("s", 1))
        rope(big1[:, QB:QB + 2048], b.io["cosqb"], b.io["s2qb"], em.val("s"))
        for tg in range(4):
            mm = em.emit("t", lambda e, tg=tg: [
                e.matmul(psA, wti[:, 2048 + k * P:2048 + (k + 1) * P],
                         big1[:, k * 2048 + tg * 512:k * 2048 + tg * 512 + 512],
                         start=(k == 0), stop=(k == 3)) for k in range(4)][-1],
                waits=[("s", em.val("s")), ("v", em.val("v"))], inc=("t", 1))
            em.emit("s", lambda e, tg=tg: e.copy(
                big1[:, KB + tg * 512:KB + tg * 512 + 512], psA),
                waits=[("t", mm)], inc=("s", 1))
        rope(big1[:, KB:KB + 2048], b.io["coskb"], b.io["s2kb"], em.val("s"))
        for tg in range(4):
            mm = em.emit("t", lambda e, tg=tg: [
                e.matmul(psA, wti[:, 2560 + k * P:2560 + (k + 1) * P],
                         big1[:, k * 2048 + tg * 512:k * 2048 + tg * 512 + 512],
                         start=(k == 0), stop=(k == 3)) for k in range(4)][-1],
                waits=[("s", em.val("s")), ("v", em.val("v"))], inc=("t", 1))
            sv = em.emit("s", lambda e: e.copy(dscb[:, 0:512], psA),
                         waits=[("t", mm)], inc=("s", 1))
            tt = em.emit("t", lambda e: [
                e.transpose(pT[:, jj * P:(jj + 1) * P],
                            dscb[:, jj * P:(jj + 1) * P], ident)
                for jj in range(4)][-1],
                waits=[("s", sv), ("v", em.val("v"))], inc=("t", 1))
            em.emit("s", lambda e, tg=tg: e.copy(
                big1[:, VT + tg * 512:VT + tg * 512 + 512], pT[:, 0:512]),
                waits=[("t", tt)], inc=("s", 1))
        s_kv = em.val("s")
        for qb in range(16):
            kr = (qb + 1) * P
            def mms(e, qb=qb, kr=kr):
                inst = None
                for ks in range(0, kr, 512):
                    w = min(512, kr - ks)
                    inst = e.matmul(
                        psc(ks, w), big1[:, QB + qb * P:QB + (qb + 1) * P],
                        big1[:, KB + ks:KB + ks + w], start=True, stop=True)
                return inst
            tsc = em.emit("t", mms,
                          waits=[("s", s_kv if qb == 0 else em.val("s")),
                                 ("v", em.val("v"))], inc=("t", 1))
            em.emit("v", lambda e, qb=qb: e.tensor_tensor(
                psc(qb * P, P), psc(qb * P, P), b.sb["tri"], ALU.add),
                waits=[("t", tsc)])
            mxc = ssq[:, 17:18]
            if kr > 1024:
                em.emit("v", lambda e: e.reduce_max(
                    ssq[:, 18:19], pC, axis=AX.X))
                em.emit("v", lambda e, kr=kr: e.reduce_max(
                    ssq[:, 19:20], pD[:, 0:kr - 1024], axis=AX.X))
                vsp(b)
                em.emit("v", lambda e: e.tensor_tensor(
                    mxc, ssq[:, 18:19], ssq[:, 19:20], ALU.max))
            else:
                em.emit("v", lambda e, kr=kr: e.reduce_max(
                    mxc, pC[:, 0:kr], axis=AX.X))
            vsp(b)
            em.emit("v", lambda e: e.tensor_scalar(
                ssq[:, 20:21], mxc, -1.0, None, ALU.mult), inc=("v", 1))
            vmx = em.val("v")
            nseg = 2 if kr > 1024 else 1
            for seg in range(nseg):
                sw = min(1024, kr - seg * 1024)
                em.emit("s", lambda e, seg=seg, sw=sw: e.activation(
                    lntm[:, seg * 1024:seg * 1024 + sw],
                    (pC if seg == 0 else pD)[:, 0:sw], AF.Exp,
                    bias=ssq[:, 20:21], scale=1.0,
                    accum_out=ssq[:, 21 + seg:22 + seg]),
                    waits=[("v", vmx)] if seg == 0 else [],
                    inc=("s", 1) if seg == nseg - 1 else None)
            sexp = em.val("s")
            if nseg == 2:
                em.emit("v", lambda e: e.tensor_add(
                    ssq[:, 21:22], ssq[:, 21:22], ssq[:, 22:23]),
                    waits=[("s", sexp)])
                vsp(b)
            em.emit("v", lambda e: e.reciprocal(ssq[:, 23:24], ssq[:, 21:22]),
                    waits=[("s", sexp)], inc=("v", 1))
            vrp = em.val("v")
            for j4 in range(0, qb + 1, 4):
                nj = min(4, qb + 1 - j4)
                tt = em.emit("t", lambda e, j4=j4, nj=nj: [
                    e.transpose(pT[:, jj * P:(jj + 1) * P],
                                lntm[:, (j4 + jj) * P:(j4 + jj + 1) * P], ident)
                    for jj in range(nj)][-1],
                    waits=[("s", em.val("s")), ("v", em.val("v"))],
                    inc=("t", 1))
                se = em.emit("s", lambda e, nj=nj: e.copy(
                    srep[:, 0:nj * P], pT[:, 0:nj * P]),
                    waits=[("t", tt)], inc=("s", 1))
                em.emit("t", lambda e, j4=j4, nj=nj, qb=qb: [
                    e.matmul(psB[:, 0:P], srep[:, jj * P:(jj + 1) * P],
                             big1[:, VT + (j4 + jj) * P:VT + (j4 + jj + 1) * P],
                             start=(j4 + jj == 0), stop=(j4 + jj == qb))
                    for jj in range(nj)][-1],
                    waits=[("s", se)], inc=("t", 1))
            sctx = em.emit("s", lambda e: e.activation(
                dscb[:, 0:P], psB[:, 0:P], AF.Identity,
                bias=0.0, scale=ssq[:, 23:24]),
                waits=[("t", em.val("t")), ("v", vrp)], inc=("s", 1))
            tt2 = em.emit("t", lambda e: e.transpose(
                pT[:, 0:P], dscb[:, 0:P], ident),
                waits=[("s", sctx)], inc=("t", 1))
            em.emit("s", lambda e, qb=qb: e.copy(
                big1[:, CX + qb * P:CX + (qb + 1) * P], pT[:, 0:P]),
                waits=[("t", tt2)], inc=("s", 1))
        em.dma("y", b.io["ptmp"][:, h * 2048:(h + 1) * 2048],
               big1[:, CX:CX + 2048], waits=[("s", em.val("s"))])
    wo3 = b.io["l1wo"].rearrange("(k p) o -> p k o", p=P)
    for k in range(16):
        em.dma("g", big2[:, k * 2048:(k + 1) * 2048],
               wo3[:, k, :], waits=[("t", em.val("t"))] if k == 0 else [])
    g_wo = em.val("gd")
    pt3 = b.io["ptmp"].rearrange("p (h t) -> p h t", t=2048)
    for tg in range(4):
        em.dma("y", big1[:, 0:8192].rearrange("p (h t) -> p h t", t=512),
               pt3[:, :, tg * 512:(tg + 1) * 512],
               waits=[("t", em.val("t"))])
        y_p = em.val("yd")
        for osl in range(16):
            def mmo2(e, osl=osl):
                return [e.matmul(
                    psA, big2[:, k * 2048 + osl * P:k * 2048 + (osl + 1) * P],
                    big1[:, k * 512:(k + 1) * 512],
                    start=(k == 0), stop=(k == 15)) for k in range(16)][-1]
            wl = [("s", em.val("s"))]
            if osl == 0:
                wl += [("yd", y_p)]
                if tg == 0:
                    wl.append(("gd", g_wo))
            tmo2 = em.emit("t", mmo2, waits=wl, inc=("t", 1))
            sev2 = em.emit("s", lambda e: e.copy(lntm[:, 0:512], psA),
                           waits=[("t", tmo2)], inc=("s", 1))
            tt2 = em.emit("t", lambda e: [
                e.transpose(pT[:, jj * P:(jj + 1) * P],
                            lntm[:, jj * P:(jj + 1) * P], ident)
                for jj in range(4)][-1],
                waits=[("s", sev2), ("v", em.val("v"))], inc=("t", 1))

            def upd(e, tg=tg, osl=osl):
                for jj in range(4):
                    e.tensor_tensor(jb[:, jj * P:(jj + 1) * P],
                                    pT[:, jj * P:(jj + 1) * P],
                                    borep[:, osl * P:(osl + 1) * P], ALU.add)
                for jj in range(4):
                    jt = tg * 4 + jj
                    e.tensor_tensor(dscb[:, jj * P:(jj + 1) * P],
                                    jb[:, jj * P:(jj + 1) * P],
                                    x[:, jt * D + osl * P:jt * D + (osl + 1) * P],
                                    ALU.add)
                inst = None
                for jj in range(4):
                    jt = tg * 4 + jj
                    inst = e.tensor_copy(
                        x[:, jt * D + osl * P:jt * D + (osl + 1) * P],
                        dscb[:, jj * P:(jj + 1) * P])
                return inst
            em.emit("v", upd, waits=[("t", tt2)], inc=("v", 1))


def build_program():
    nc = bass.Bass("TRN2", target_bir_lowering=False, debug=False)
    nc.detect_race_conditions = False
    em = Em(nc)
    b = alloc_all(nc, em)
    register_consts(b)
    em.emit("v", lambda e: e.memset(b.sb["spc"][:, :], 0.0))
    ready = load_consts_and_x(b)
    for qn in "tvsg":
        em.emit(qn, None, waits=[("yd", ready["yd"]), ("gd", ready["gd"])])
    prep = l0ffn_weight_prep(b, ready)
    if N_SUBLAYERS >= 1:
        l0ffn_call(b, prep, PANEL_LIST or [0, 1024], 0.5)
    if N_SUBLAYERS >= 2:
        l0attn_call(b)
    if N_SUBLAYERS >= 3:
        l0ffn_call(b, prep, PANEL_LIST or [0, 1024], 0.5)
    if N_SUBLAYERS >= 4:
        l1ffn_call(b, 0.5)
    if N_SUBLAYERS >= 5:
        l1attn_call(b)
    if N_SUBLAYERS >= 6:
        l1ffn_call(b, 0.5)

    vfin = em.val("v")
    for j in range(NT):
        w = [("v", vfin)] if j == 0 else [("yd", em.val("yd"))]
        sc = em.emit("s", lambda e, j=j: e.copy(
            b.sb["outst"][:, :], b.sb["x"][:, j * D:(j + 1) * D]),
            waits=w, inc=("s", 1))
        em.dma("y", b.io["xout"][j * P:(j + 1) * P, :], b.sb["outst"][:, :],
               waits=[("s", sc)])
    em.emit("y", None, waits=[("yd", em.val("yd"))])

    with nc.Block() as block:
        em.replay(block)
    return nc


_NC_CACHE = None


def shared_inputs(inputs):
    tabs = host_tables()
    m = dict(tabs)
    m["wupT"] = np.ascontiguousarray(inputs["L0_ffn_w_up"].T)
    m["wdownT"] = np.ascontiguousarray(inputs["L0_ffn_w_down"].T)
    m["wqkvuT"] = np.ascontiguousarray(inputs["L0_attn_W_qkvu"].T)
    m["woutT"] = np.ascontiguousarray(inputs["L0_attn_W_out"].T)
    m["slope"] = np.ascontiguousarray(
        inputs["L0_attn_slope_rate"].astype(np.float32, copy=False)
        .reshape(1, 16))
    m["l1wg"] = np.ascontiguousarray(inputs["L1_ffn_w_gate"])
    m["l1wu"] = np.ascontiguousarray(inputs["L1_ffn_w_up"])
    m["l1wd"] = np.ascontiguousarray(inputs["L1_ffn_w_down"])
    m["l1sg"] = np.ascontiguousarray(
        inputs["L1_ffn_s_gate"].astype(np.float32, copy=False).reshape(64, P).T)
    m["l1su"] = np.ascontiguousarray(
        inputs["L1_ffn_s_up"].astype(np.float32, copy=False).reshape(64, P).T)
    m["l1sd"] = np.ascontiguousarray(
        inputs["L1_ffn_s_down"].astype(np.float32, copy=False).reshape(1, D))
    m["l1wq"] = np.ascontiguousarray(inputs["L1_attn_Wq"])
    m["l1wdn"] = np.ascontiguousarray(inputs["L1_attn_Wdown"])
    m["l1wk"] = np.ascontiguousarray(inputs["L1_attn_Wk_up"])
    m["l1wv"] = np.ascontiguousarray(inputs["L1_attn_Wv_up"])
    m["l1wo"] = np.ascontiguousarray(inputs["L1_attn_Wo"])
    m["l1bq"] = np.ascontiguousarray(
        inputs["L1_attn_bq"].astype(np.float32, copy=False).reshape(16, P).T)
    m["l1bo"] = np.ascontiguousarray(
        inputs["L1_attn_bo"].astype(np.float32, copy=False).reshape(1, D))
    return m


def kernel(**inputs):
    global _NC_CACHE
    if _NC_CACHE is None:
        _NC_CACHE = build_program()
    nc = _NC_CACHE
    shared = shared_inputs(inputs)
    xfull = np.ascontiguousarray(inputs["L0_attn_x"].astype(np.float32, copy=False))
    in_maps = []
    for c in range(8):
        bq = c // 2
        m = dict(shared)
        m["xin"] = np.ascontiguousarray(xfull[bq])
        in_maps.append(m)
    res = run_bass_kernel_spmd(nc, in_maps, core_ids=list(range(8)))
    out = np.empty((4, NTOK, D), dtype=np.float32)
    for bq in range(4):
        out[bq, :TSPLIT] = res.results[2 * bq]["xout"][:TSPLIT]
        out[bq, TSPLIT:] = res.results[2 * bq + 1]["xout"][TSPLIT:]
    return out
